# version 15
# baseline (speedup 1.0000x reference)
import numpy as np
import ml_dtypes
import concourse.bass as bass
import concourse.mybir as mybir
from concourse.bass_utils import run_bass_kernel_spmd
from contextlib import ExitStack

F32 = mybir.dt.float32
BF16 = mybir.dt.bfloat16
AF = mybir.ActivationFunctionType
ALU = mybir.AluOpType
AX = mybir.AxisListType

S = 2048
D = 2048
H = 16
DH = 128
NT = 16
NCH = 16
EXP_BIAS = -20.0
EPS = 1e-5
QSCALE = float(1.0 / np.sqrt(128.0))

ENGS = ("sync", "gpsimd", "vector", "scalar", "tensor")


class Ctr:
    def __init__(self, nc, name):
        self.name = name
        self.h = nc.alloc_semaphore(name)
        self.v = 0

    def inc(self, k=1):
        self.v += k
        return self.v


class Builder:
    def __init__(self, nc):
        self.nc = nc
        self.ops = {e: [] for e in ENGS}
        self.waited = {e: {} for e in ENGS}
        self.c = {n: Ctr(nc, n) for n in
                  ("ld_c", "ld_x0", "ld_x1", "ld_w0", "ld_w1", "ld_wo", "ld_wt",
                   "ldg", "out0", "out1", "pe", "act", "dve", "pool")}
        self.guard = {}

    def emit(self, eng, fn, waits=(), inc=None, inck=1):
        waits = [(cc, v) for (cc, v) in waits if cc is not None and v > 0]
        dedup = self.waited[eng]
        real = []
        for (cc, v) in waits:
            if dedup.get(cc.name, 0) < v:
                dedup[cc.name] = v
                real.append((cc.h, v))
        inch = None
        if inc is not None:
            inc.inc(inck)
            inch = inc.h

        def thunk(e, _w=tuple(real), _inch=inch, _fn=fn, _k=inck):
            for (h, v) in _w:
                e.wait_ge(h, v)
            ins = _fn(e)
            if _inch is not None:
                ins.then_inc(_inch, _k)
        self.ops[eng].append(thunk)
        return inc.v if inc is not None else None

    def gw(self, keys):
        ws = {}
        for k in keys:
            g = self.guard.get(k)
            if g is not None:
                cc, v = g
                if ws.get(cc.name, (cc, 0))[1] < v:
                    ws[cc.name] = (cc, v)
        return list(ws.values())

    def sg(self, keys, ctr, value):
        for k in keys:
            self.guard[k] = (ctr, value)

    def run(self, block):
        for eng in ENGS:
            lst = self.ops[eng]

            def body(e, _lst=lst):
                for fn in _lst:
                    fn(e)
            getattr(block, eng)(body)


def build_program(upto="full"):
    nc = bass.Bass("TRN2", target_bir_lowering=False,
                   dynamic_dma_scratch_size=16384)

    x_in = nc.dram_tensor("x", [S, D], F32, kind="ExternalInput").ap()
    wqkv = nc.dram_tensor("wqkv", [H, D, 384], F32, kind="ExternalInput").ap()
    w_out = nc.dram_tensor("w_out", [D, D], F32, kind="ExternalInput").ap()
    weightT = nc.dram_tensor("weightT", [D, D], F32, kind="ExternalInput").ap()
    routerT = nc.dram_tensor("routerT", [D, 8], F32, kind="ExternalInput").ap()
    at_in = nc.dram_tensor("at", [D, 128], F32, kind="ExternalInput").ap()
    bflat_in = nc.dram_tensor("bflat", [128, D], F32, kind="ExternalInput").ap()
    out = nc.dram_tensor("out", [S, D], F32, kind="ExternalOutput").ap()
    dbg = None
    if upto != "full":
        dbg = nc.dram_tensor("dbg", [4, 128, 32768], BF16,
                             kind="ExternalOutput").ap()

    cvals = np.zeros((128, 8), np.float32)
    cvals[:, 0] = EXP_BIAS
    cvals[:, 1] = EPS
    cpack = np.concatenate([
        np.eye(128, dtype=ml_dtypes.bfloat16).view(np.uint8),
        np.triu(np.ones((128, 128))).astype(ml_dtypes.bfloat16).view(np.uint8),
        cvals.view(np.uint8)], axis=1)
    consts_t = nc.inline_tensor(np.ascontiguousarray(cpack), "consts")

    with ExitStack() as st0:
        en = st0.enter_context
        A1 = en(nc.sbuf_tensor("A1", [128, 32768], BF16))
        A2 = en(nc.sbuf_tensor("A2", [128, 32768], BF16))
        cbuf = en(nc.sbuf_tensor("cbuf", [128, 544], mybir.dt.uint8))
        sid = cbuf[:, 0:256].bitcast(BF16)
        stri = cbuf[:, 256:512].bitcast(BF16)
        scon = cbuf[:, 512:544].bitcast(F32)
        sst = en(nc.sbuf_tensor("sst", [128, 4 * 40], F32))
        srcp = en(nc.sbuf_tensor("srcp", [128, 4], F32))

        B = Builder(nc)
        c = B.c
        EBIAS = scon[:, 0:1]
        EPSC = scon[:, 1:2]

        B.emit("sync", lambda s: s.dma_start(out=cbuf[:], in_=consts_t.ap()),
               inc=c["ld_c"], inck=16)
        consts_c = (c["ld_c"], 16)

        def stc(par, i0, i1=None):
            base = 40 * par
            if i1 is None:
                i1 = i0 + 1
            return sst[:, base + i0: base + i1]

        def plan_ln(src_fn, par, pre_waits, sq_scratch=None,
                    psum_src=False):
            base = 40 * par
            full = src_fn(0, 2048) if True else None
            fl = B.emit("vector", lambda e, _f=full, _o=stc(par, 32):
                        e.reduce_sum(_o, _f, axis=AX.X),
                        waits=list(pre_waits) + B.gw([("sst", par)]),
                        inc=c["dve"])
            sq = B.emit("scalar", lambda e, _f=full, _s=sq_scratch,
                        _o=stc(par, 34): e.activation(
                            _s, _f, AF.Square, accum_out=_o),
                        waits=list(pre_waits) + B.gw([("sstA", par)]) +
                        ([(c["dve"], fl)] if psum_src else []),
                        inc=c["act"])
            fl = B.emit("vector", lambda e, _i=stc(par, 32), _o=stc(par, 35):
                        e.tensor_scalar(_o, _i, 1.0 / 2048.0, None, ALU.mult),
                        waits=[(c["dve"], fl), (c["act"], sq)], inc=c["dve"])
            fl = B.emit("vector", lambda e, _i=stc(par, 34), _o=stc(par, 39):
                        e.tensor_scalar(_o, _i, 1.0 / 2048.0, None, ALU.mult),
                        waits=[(c["dve"], fl)], inc=c["dve"])
            fl = B.emit("vector", lambda e, _m=stc(par, 35), _t=stc(par, 39),
                        _o=stc(par, 36):
                        e.scalar_tensor_tensor(_o, _m, stc(par, 35), _t,
                                               ALU.mult, ALU.subtract),
                        waits=[(c["dve"], fl)], inc=c["dve"])
            lnv = B.emit("scalar", lambda e, _i=stc(par, 36), _o=stc(par, 39):
                         e.activation(_o, _i, AF.Ln, bias=EPSC, scale=-1.0),
                         waits=[(c["dve"], fl), consts_c], inc=c["act"])
            av = B.emit("scalar", lambda e, _i=stc(par, 39), _o=stc(par, 37):
                        e.activation(_o, _i, AF.Exp, scale=-0.5),
                        waits=[(c["act"], lnv)], inc=c["act"])
            dv = B.emit("vector", lambda e, _m=stc(par, 35), _o=stc(par, 38):
                        e.tensor_scalar(_o, _m, stc(par, 37), -1.0,
                                        ALU.mult, ALU.mult),
                        waits=[(c["act"], av)], inc=c["dve"])
            B.sg([("sst", par)], c["dve"], dv)
            B.sg([("sstA", par)], c["act"], av)
            return stc(par, 37), stc(par, 38), dv

        with ExitStack() as ph:
            pen = ph.enter_context
            xln = pen(nc.sbuf_tensor("xln", [128, 2 * 2048], BF16))
            wbf = pen(nc.sbuf_tensor("wbf", [128, 2 * 6144], BF16))
            Qbf = pen(nc.sbuf_tensor("Qbf", [128, 2 * 2048], BF16))
            Kbf = pen(nc.sbuf_tensor("Kbf", [128, 2 * 2048], BF16))
            VTs = pen(nc.sbuf_tensor("VTs", [128, 2 * 2048], BF16))
            att = pen(nc.sbuf_tensor("att", [128, 4 * 128], BF16))
            Vaug = pen(nc.sbuf_tensor("Vaug", [128, 16 * 129], BF16))
            expT = pen(nc.sbuf_tensor("expT", [128, 16 * 512], BF16))
            xio2slot = expT[:, 0:8192].bitcast(F32)
            P_qkv = pen(nc.psum_tensor("P_qkv", [128, 2048], F32))
            P_sc = pen(nc.psum_tensor("P_sc", [128, 1024], F32))
            P_t1 = pen(nc.psum_tensor("P_t1", [128, 1024], BF16))
            P_t2 = pen(nc.psum_tensor("P_t2", [128, 1024], BF16))

            ln1_slots = [P_qkv[:, 0:512].bitcast(BF16),
                         P_qkv[:, 512:1024].bitcast(BF16),
                         P_qkv[:, 1024:1536].bitcast(BF16),
                         P_qkv[:, 1536:2048].bitcast(BF16),
                         P_sc[:, 0:512].bitcast(BF16),
                         P_sc[:, 512:1024].bitcast(BF16),
                         P_t1, P_t2]
            for t in range(NT):
                par = t % 4
                bpar = t % 2
                xio = xio2slot[:, 2048 * bpar:2048 * bpar + 2048]
                xsem = c["ld_x0"] if bpar == 0 else c["ld_x1"]
                xrdy = B.emit("sync", lambda s, _t=t, _x=xio: s.dma_start(
                    out=_x, in_=x_in[128 * _t:128 * _t + 128, :]),
                    waits=B.gw([("xio", bpar)]), inc=xsem, inck=16)
                rstd, negmr, dve_ln = plan_ln(
                    (lambda j, w, _x=xio: _x[:, j:j + w]), par,
                    pre_waits=[(xsem, xrdy)],
                    sq_scratch=Vaug[:, 0:2048])
                xlv = B.emit("scalar", lambda e, _p=bpar, _r=rstd, _m=negmr,
                             _x=xio: e.activation(
                                 xln[:, 2048 * _p:2048 * _p + 2048],
                                 _x, AF.Identity, bias=_m, scale=_r),
                             waits=[(c["dve"], dve_ln)] + B.gw([("xln", bpar)]),
                             inc=c["act"])
                B.sg([("xio", bpar)], c["act"], xlv)
                for grp in range(4):
                    bank = (4 * t + grp) % 8
                    slot = ln1_slots[bank]
                    gwb = B.gw([bank])
                    for j in range(4):
                        ch = 4 * grp + j
                        B.emit("tensor", lambda e, _p=bpar, _c=ch, _j=j,
                               _pt=slot: e.transpose(
                                   _pt[:, 128 * _j:128 * _j + 128],
                                   xln[:, 2048 * _p + 128 * _c:
                                       2048 * _p + 128 * _c + 128], sid),
                               waits=([(c["act"], xlv), consts_c] + gwb)
                               if j == 0 else (),
                               inc=(c["pe"] if j == 3 else None))
                    tv = c["pe"].v
                    dv = B.emit("vector", lambda e, _t=t, _g=grp, _pt=slot:
                                e.tensor_copy(
                                    A1[:].rearrange("p (c f) -> p c f",
                                                    f=2048)
                                    [:, 4 * _g:4 * _g + 4,
                                     128 * _t:128 * _t + 128],
                                    _pt[:, 0:512].rearrange(
                                        "p (c f) -> p c f", f=128)),
                                waits=[(c["pe"], tv)], inc=c["dve"])
                    B.sg([bank], c["dve"], dv)
                B.sg([("xln", bpar)], c["pe"], c["pe"].v)
            xlnT_dve = c["dve"].v
            vaug_ones = B.emit("vector", lambda e: e.memset(Vaug[:], 1.0),
                               waits=[(c["act"], c["act"].v)], inc=c["dve"])
            if upto == "ln1":
                dump_and_finish([(A1[:],)])
                with nc.Block() as block:
                    B.run(block)
                return nc

            vt_blocks = [[0] * 4, [0] * 4]

            def plan_wdma(hh):
                wpar = hh % 2
                wsem = c["ld_w0"] if wpar == 0 else c["ld_w1"]
                wrdy = B.emit("gpsimd", lambda s, _h=hh, _p=wpar: s.dma_start(
                    out=wbf[:, 6144 * _p:6144 * _p + 6144]
                    .rearrange("p (x f) -> p x f", f=384),
                    in_=wqkv[_h].rearrange("(x p) f -> p x f", p=128)),
                    waits=B.gw([("wbf", wpar)]), inc=wsem, inck=16)
                return wsem, wrdy

            def make_qkv_units(hh, wsem, wrdy):
                wpar = hh % 2
                units = []
                gidx = [0]

                def mk(m, half, tb, first):
                    col0 = 1024 * half + 512 * tb

                    def unit():
                        bank = gidx[0] % 2
                        gidx[0] += 1
                        gw = B.gw([bank])
                        for ch in range(NCH):
                            wts = ()
                            if ch == 0:
                                wts = gw + ([(wsem, wrdy),
                                             (c["dve"], xlnT_dve)]
                                            if first else [])
                            B.emit("tensor", lambda e, _c=ch, _m=m, _q=col0,
                                   _p=wpar, _b=bank: e.matmul(
                                       P_qkv[:, 512 * _b:512 * _b + 512],
                                       wbf[:, 6144 * _p + 384 * _c
                                           + 128 * _m:
                                           6144 * _p + 384 * _c
                                           + 128 * _m + 128],
                                       A1[:, 2048 * _c + _q:
                                           2048 * _c + _q + 512],
                                       start=(_c == 0), stop=(_c == 15)),
                                   waits=wts,
                                   inc=(c["pe"] if ch == 15 else None))
                        mmv = c["pe"].v
                        dst = (Qbf, Kbf, VTs)[m]
                        if m == 0:
                            ev = B.emit("vector", lambda e, _q=col0, _b=bank,
                                        _hb=wpar: e.tensor_scalar(
                                            Qbf[:, 2048 * _hb + _q:
                                                2048 * _hb + _q + 512],
                                            P_qkv[:, 512 * _b:512 * _b + 512],
                                            QSCALE, None, ALU.mult),
                                        waits=[(c["pe"], mmv)], inc=c["dve"])
                        else:
                            ev = B.emit("vector", lambda e, _q=col0, _d=dst,
                                        _b=bank, _hb=wpar: e.tensor_copy(
                                            _d[:, 2048 * _hb + _q:
                                                2048 * _hb + _q + 512],
                                            P_qkv[:, 512 * _b:512 * _b + 512]),
                                        waits=[(c["pe"], mmv)], inc=c["dve"])
                        B.sg([bank], c["dve"], ev)
                        if m == 2:
                            vt_blocks[wpar][2 * half + tb] = ev
                        if m == 2 and half == 1 and tb == 1:
                            B.sg([("wbf", wpar)], c["pe"], c["pe"].v)
                    return unit

                first = True
                for m in range(3):
                    for half in range(2):
                        for tb in range(2):
                            units.append(mk(m, half, tb, first))
                            first = False
                return units

            wsem0, wrdy0 = plan_wdma(0)
            for u in make_qkv_units(0, wsem0, wrdy0):
                u()

            for h in range(H):
                hb = h % 2
                for grp in range(2):
                    pt = P_t1 if grp == 0 else P_t2
                    bank = 6 + grp
                    gwb = B.gw([bank])
                    for j in range(8):
                        k = 8 * grp + j
                        B.emit("tensor", lambda e, _k=k, _j=j, _pt=pt,
                               _hb=hb: e.transpose(
                                   _pt[:, 128 * _j:128 * _j + 128],
                                   VTs[:, 2048 * _hb + 128 * _k:
                                       2048 * _hb + 128 * _k + 128], sid),
                               waits=([(c["dve"],
                                        max(vt_blocks[hb][2 * grp],
                                            vt_blocks[hb][2 * grp + 1]))]
                                      + gwb
                                      + ([(c["dve"], vaug_ones)]
                                         if h == 0 and grp == 0 else []))
                               if j == 0 else (),
                               inc=(c["pe"] if j == 7 else None))
                    tv = c["pe"].v
                    av = B.emit("vector", lambda e, _g=grp, _pt=pt:
                                e.tensor_copy(
                                    Vaug[:].rearrange("p (k f) -> p k f",
                                                      f=129)
                                    [:, 8 * _g:8 * _g + 8, 0:128],
                                    _pt[:, 0:1024].rearrange(
                                        "p (k f) -> p k f", f=128)),
                                waits=[(c["pe"], tv)], inc=c["dve"])
                    B.sg([bank], c["dve"], av)
                vaug_dve = c["dve"].v

                if h == 0:
                    pend_T = []

                    def drain_T(item):
                        _h, _qt, _p, _attv = item
                        pt2 = P_t1 if _qt % 2 == 0 else P_t2
                        pbank2 = 6 + _qt % 2
                        tv2 = B.emit("tensor", lambda e, _pt=pt2, _pp=_p:
                                     e.transpose(_pt[:, 0:128],
                                                 att[:, 128 * _pp:
                                                     128 * _pp + 128], sid),
                                     waits=[(c["dve"], _attv)]
                                     + B.gw([pbank2]), inc=c["pe"])
                        B.sg([("att", _p)], c["pe"], tv2)
                        av2 = B.emit("vector", lambda e, _hh=_h, _q2=_qt,
                                     _pt=pt2: e.tensor_copy(
                                         A2[:, 2048 * _hh + 128 * _q2:
                                             2048 * _hh + 128 * _q2 + 128],
                                         _pt[:, 0:128]),
                                     waits=[(c["pe"], tv2)], inc=c["dve"])
                        B.sg([pbank2], c["dve"], av2)

                if h + 1 < H:
                    wsemN, wrdyN = plan_wdma(h + 1)
                    units = make_qkv_units(h + 1, wsemN, wrdyN)
                else:
                    units = []
                ui = [0]

                def pump(n):
                    for _ in range(n):
                        if ui[0] < len(units):
                            units[ui[0]]()
                            ui[0] += 1

                for qb in range(4):
                    exp_done = {}
                    mask_done = {}
                    for i0 in range(0, 4 * qb, 2):
                        gwp = B.gw([4, 5])
                        for d in range(2):
                            B.emit("tensor", lambda e, _i=i0 + d, _qb=qb,
                                   _d=d, _hb=hb: e.matmul(
                                       P_sc[:, 512 * _d:512 * _d + 512],
                                       Kbf[:, 2048 * _hb + 128 * _i:
                                           2048 * _hb + 128 * _i + 128],
                                       Qbf[:, 2048 * _hb + 512 * _qb:
                                           2048 * _hb + 512 * _qb + 512],
                                       start=True, stop=True),
                                   waits=(gwp + [(c["dve"], vaug_dve)])
                                   if d == 0 else (),
                                   inc=(c["pe"] if d == 1 else None))
                        scv = c["pe"].v
                        ev = B.emit("scalar", lambda e, _i=i0:
                                    e.activation(
                                        expT[:, 512 * _i:512 * _i + 1024],
                                        P_sc[:, 0:1024], AF.Exp,
                                        bias=EBIAS, scale=1.0),
                                    waits=[(c["pe"], scv)], inc=c["act"])
                        exp_done[i0] = ev
                        exp_done[i0 + 1] = ev
                        B.sg([4, 5], c["act"], ev)
                        pump(1)
                    for s in range(4):
                        i = 4 * qb + s
                        sub = s % 2
                        bank = 4 + sub
                        n = 512 - 128 * s
                        scv = B.emit("tensor", lambda e, _i=i, _qb=qb,
                                     _u=sub, _o=128 * s, _hb=hb:
                                     e.matmul(
                                         P_sc[:, 512 * _u + _o:
                                             512 * _u + 512],
                                         Kbf[:, 2048 * _hb + 128 * _i:
                                             2048 * _hb + 128 * _i + 128],
                                         Qbf[:, 2048 * _hb + 512 * _qb + _o:
                                             2048 * _hb + 512 * _qb + 512],
                                         start=True, stop=True),
                                     waits=B.gw([bank]) +
                                     [(c["dve"], vaug_dve)], inc=c["pe"])
                        ev = B.emit("scalar", lambda e, _i=i, _u=sub,
                                    _o=128 * s: e.activation(
                                        expT[:, 512 * _i + _o:
                                            512 * _i + 512],
                                        P_sc[:, 512 * _u + _o:
                                            512 * _u + 512],
                                        AF.Exp, bias=EBIAS, scale=1.0),
                                    waits=[(c["pe"], scv)], inc=c["act"])
                        exp_done[i] = ev
                        B.sg([bank], c["act"], ev)
                    for w in range(2):
                        base = 2048 * qb + 1280 * w
                        ones_ap = expT[:, base:base + 768]
                        dv3 = bass.AP(ones_ap.tensor, ones_ap.offset,
                                      [list(p) for p in ones_ap.ap[:1]] +
                                      [[640, 2], [1, 128]])
                        pv = B.emit("vector", lambda e, _v=dv3:
                                    e.tensor_tensor(
                                        _v, _v,
                                        stri[:].unsqueeze(1)
                                        .broadcast_to([128, 2, 128]),
                                        ALU.mult),
                                    waits=[(c["act"],
                                            exp_done[4 * qb + 2 * w + 1])],
                                    inc=c["dve"])
                        mask_done[4 * qb + 2 * w] = pv
                        mask_done[4 * qb + 2 * w + 1] = pv
                    for u2 in range(4):
                        qt = 4 * qb + u2
                        avsl = 1024 + 512 * (qt % 2)
                        bank = 2 + qt % 2
                        wts = B.gw([bank]) + [(c["act"], exp_done[qt]),
                                              (c["dve"], vaug_dve)]
                        if qt in mask_done:
                            wts.append((c["dve"], mask_done[qt]))
                        for i in range(qt + 1):
                            B.emit("tensor", lambda e, _i=i, _u=u2, _a=avsl,
                                   _q=qt: e.matmul(
                                       P_qkv[:, _a:_a + 129],
                                       expT[:, 512 * _i + 128 * _u:
                                           512 * _i + 128 * _u + 128],
                                       Vaug[:, 129 * _i:129 * _i + 129],
                                       start=(_i == 0), stop=(_i == _q)),
                                   waits=(wts if i == 0 else ()),
                                   inc=(c["pe"] if i == qt else None))
                        avv = c["pe"].v
                        apar = qt % 4
                        fv = B.emit("vector", lambda e, _a=avsl, _p=apar:
                                    e.reciprocal(
                                        srcp[:, _p:_p + 1],
                                        P_qkv[:, _a + 128:_a + 129]),
                                    waits=[(c["pe"], avv)]
                                    + B.gw([("rcp", apar)]), inc=c["dve"])
                        attv = B.emit("vector", lambda e, _a=avsl, _p=apar:
                                      e.tensor_scalar(
                                          att[:, 128 * _p:128 * _p + 128],
                                          P_qkv[:, _a:_a + 128],
                                          srcp[:, _p:_p + 1], None,
                                          ALU.mult),
                                      waits=[(c["dve"], fv)]
                                      + B.gw([("att", apar)]),
                                      inc=c["dve"])
                        B.sg([("rcp", apar)], c["dve"], attv)
                        B.sg([bank], c["dve"], attv)
                        pend_T.append((h, qt, apar, attv))
                        if len(pend_T) > 3:
                            drain_T(pend_T.pop(0))
                pump(12)
            attn_act = c["act"].v
            while pend_T:
                drain_T(pend_T.pop(0))
            attn_act = c["act"].v
            attn_pe = c["pe"].v
            attn_dve = c["dve"].v
            attn_pool = c["pool"].v
            if upto == "qkv1":
                dump_and_finish([(Qbf[:],), (Kbf[:],), (Vaug[:].rearrange("p x -> p x") if False else expT[:],)])
                with nc.Block() as block:
                    B.run(block)
                return nc
            if upto == "attn":
                dump_and_finish([(A2[:],)])
                with nc.Block() as block:
                    B.run(block)
                return nc

        with ExitStack() as ph:
            pen = ph.enter_context
            wo_bf = pen(nc.sbuf_tensor("wo_bf", [128, 32768], BF16))
            xio2 = pen(nc.sbuf_tensor("xio2", [128, 2048], F32))
            ntmp = pen(nc.sbuf_tensor("ntmp", [128, 2048], BF16))
            P_e = pen(nc.psum_tensor("P_e", [128, 2048], F32))
            P_o = pen(nc.psum_tensor("P_o", [128, 2048], F32))

            scope_w = [(c["pe"], attn_pe), (c["act"], attn_act),
                       (c["dve"], attn_dve), (c["pool"], attn_pool)]
            for hh in range(2):
                B.emit("gpsimd", lambda s, _h=hh: s.dma_start(
                    out=wo_bf[:, 16384 * _h:16384 * _h + 16384]
                    .rearrange("p (x f) -> p x f", f=2048),
                    in_=w_out[1024 * _h:1024 * _h + 1024, :]
                    .rearrange("(x p) f -> p x f", p=128)),
                    waits=(scope_w if hh == 0 else []),
                    inc=c["ld_wo"], inck=16)
            wo_all = c["ld_wo"].v

            for t in range(NT):
                par = t % 2
                P = P_e if par == 0 else P_o
                banks = [0, 1, 2, 3] if par == 0 else [4, 5, 6, 7]
                gww = B.gw(banks)
                xrdy = B.emit("gpsimd", lambda e, _t=t: e.dma_start(
                    out=xio2[:], in_=x_in[128 * _t:128 * _t + 128, :]),
                    waits=B.gw(["xio2"]), inc=c["ldg"], inck=16)
                for ch in range(NCH):
                    wts = ()
                    if ch == 0:
                        wts = gww + [(c["ld_wo"], wo_all),
                                     (c["act"], attn_act)]
                    for f in range(4):
                        B.emit("tensor", lambda e, _c=ch, _t=t, _f=f, _P=P:
                               e.matmul(
                                   _P[:, 512 * _f:512 * _f + 512],
                                   A2[:, 2048 * _c + 128 * _t:
                                       2048 * _c + 128 * _t + 128],
                                   wo_bf[:, 2048 * _c + 512 * _f:
                                       2048 * _c + 512 * _f + 512],
                                   start=(_c == 0), stop=(_c == 15)),
                               waits=(wts if f == 0 else ()),
                               inc=(c["pe"] if (ch == 15 and f == 3) else None))
                mmv = c["pe"].v
                rstd, negmr, dve_ln = plan_ln(
                    (lambda j, w, _P=P: _P[:, j:j + w]), par,
                    pre_waits=[(c["pe"], mmv)] + B.gw(["ntmp"]),
                    sq_scratch=ntmp[:], psum_src=True)
                nv = B.emit("scalar", lambda e, _P=P, _r=rstd, _m=negmr:
                            e.activation(ntmp[:], _P[:], AF.Identity,
                                         bias=_m, scale=_r),
                            waits=[(c["dve"], dve_ln)] + B.gw(["ntmp"]),
                            inc=c["act"])
                x2v = B.emit("vector", lambda e, _t=t: e.tensor_tensor(
                    A1[:, 2048 * _t:2048 * _t + 2048], ntmp[:], xio2[:],
                    ALU.add),
                    waits=[(c["act"], nv), (c["ldg"], xrdy)], inc=c["dve"])
                B.sg(banks, c["dve"], x2v)
                B.sg(["xio2"], c["dve"], x2v)
                B.sg(["ntmp"], c["dve"], x2v)
                B.sg([("x2", t)], c["dve"], x2v)
            outproj_pe = c["pe"].v
            outproj_dve = c["dve"].v
            outproj_act = c["act"].v
            if upto == "outproj":
                dump_and_finish([(A1[:],), (A2[:],), (wo_bf[:],)])
                with nc.Block() as block:
                    B.run(block)
                return nc

        with ExitStack() as ph:
            pen = ph.enter_context
            xln2 = pen(nc.sbuf_tensor("xln2", [128, 2 * 2048], BF16))
            x2T = pen(nc.sbuf_tensor("x2T", [128, 2 * 2048], BF16))
            sbf = pen(nc.sbuf_tensor("sbf", [128, 2048], BF16))
            sat = pen(nc.sbuf_tensor("sat", [128, 2048], BF16))
            srt = pen(nc.sbuf_tensor("srt", [128, 128], BF16))
            sff = pen(nc.sbuf_tensor("sff", [128, 2 * 2048], F32))
            sqs = pen(nc.sbuf_tensor("sqs", [128, 1024], F32))
            x2s = pen(nc.sbuf_tensor("x2s", [128, 2048], F32))
            ost = pen(nc.sbuf_tensor("ost", [128, 2 * 2048], F32))
            stop8 = pen(nc.sbuf_tensor("stop8", [128, 8], F32))
            ssel = pen(nc.sbuf_tensor("ssel", [128, 8], F32))
            se8 = pen(nc.sbuf_tensor("se8", [128, 8], F32))
            sw8 = pen(nc.sbuf_tensor("sw8", [128, 8], F32))
            snm = pen(nc.sbuf_tensor("snm", [128, 2], F32))
            su = pen(nc.sbuf_tensor("su", [128, 128], BF16))
            sut = pen(nc.sbuf_tensor("sut", [128, 128], BF16))
            P_b = pen(nc.psum_tensor("P_b", [128, 2048], F32))
            P_x = pen(nc.psum_tensor("P_x", [128, 1024], F32))
            P_u1 = pen(nc.psum_tensor("P_u1", [128, 1024], BF16))
            P_u2 = pen(nc.psum_tensor("P_u2", [128, 1024], BF16))

            scope2 = [(c["pe"], outproj_pe), (c["dve"], outproj_dve),
                      (c["act"], outproj_act)]
            for hh in range(2):
                B.emit("gpsimd", lambda s, _h=hh: s.dma_start(
                    out=A2[:, 16384 * _h:16384 * _h + 16384]
                    .rearrange("p (x f) -> p x f", f=2048),
                    in_=weightT[1024 * _h:1024 * _h + 1024, :]
                    .rearrange("(x p) f -> p x f", p=128)),
                    waits=(scope2 if hh == 0 else []),
                    inc=c["ld_wt"], inck=16)
            B.emit("gpsimd", lambda s: s.dma_start(
                out=srt[:].rearrange("p (x e) -> p x e", e=8),
                in_=routerT.rearrange("(x p) e -> p x e", p=128)),
                inc=c["ld_wt"], inck=16)
            B.emit("gpsimd", lambda s: s.dma_start(
                out=sat[:].rearrange("p (x f) -> p x f", f=128),
                in_=at_in.rearrange("(x p) f -> p x f", p=128)),
                inc=c["ld_wt"], inck=16)
            B.emit("gpsimd", lambda s: s.dma_start(
                out=sbf[:], in_=bflat_in), inc=c["ld_wt"], inck=16)
            wT_all = c["ld_wt"].v

            def plan_ln2(t):
                par = t % 2
                rstd, negmr, dve_ln = plan_ln(
                    (lambda j, w, _t=t: A1[:, 2048 * _t + j:
                                           2048 * _t + j + w]),
                    par, pre_waits=B.gw([("x2", t)]),
                    sq_scratch=sqs[:].bitcast(BF16))
                return B.emit("scalar", lambda e, _p=par, _t=t, _r=rstd,
                              _m=negmr: e.activation(
                                  xln2[:, 2048 * _p:2048 * _p + 2048],
                                  A1[:, 2048 * _t:2048 * _t + 2048],
                                  AF.Identity, bias=_m, scale=_r),
                              waits=[(c["dve"], dve_ln)]
                              + B.gw([("xln2", par)]), inc=c["act"])

            xl2_next = plan_ln2(0)
            x2T_ready = [0] * NCH

            def plan_T(t, xl2):
                par = t % 2
                for grp in range(2):
                    pt = P_u1 if grp == 0 else P_u2
                    bank = 6 + grp
                    gwb = B.gw([bank])
                    for j in range(8):
                        ch = 8 * grp + j
                        B.emit("tensor", lambda e, _p=par, _c=ch, _j=j,
                               _pt=pt: e.transpose(
                                   _pt[:, 128 * _j:128 * _j + 128],
                                   xln2[:, 2048 * _p + 128 * _c:
                                       2048 * _p + 128 * _c + 128], sid),
                               waits=([(c["act"], xl2)] + gwb) if j == 0
                               else (), inc=(c["pe"] if j == 7 else None))
                    tv = c["pe"].v
                    av = B.emit("scalar", lambda e, _p=par, _g=grp, _pt=pt:
                                e.activation(
                                    x2T[:, 2048 * _p + 1024 * _g:
                                        2048 * _p + 1024 * _g + 1024],
                                    _pt[:, 0:1024], AF.Copy),
                                waits=[(c["pe"], tv)], inc=c["act"])
                    B.sg([bank], c["act"], av)
                    for j in range(8):
                        x2T_ready[8 * grp + j] = av
                B.sg([("xln2", par)], c["pe"], c["pe"].v)

            plan_T(0, xl2_next)
            for t in range(NT):
                par = t % 2
                aux0 = 512 * par
                auxbank = 4 + par
                gw_aux = B.gw([auxbank])
                for ch in range(NCH):
                    wts = [(c["act"], x2T_ready[ch])]
                    if ch == 0:
                        wts += gw_aux + [(c["ld_wt"], wT_all)]
                    B.emit("tensor", lambda e, _c=ch, _p=par, _a=aux0:
                           e.matmul(P_x[:, _a:_a + 8],
                                    x2T[:, 2048 * _p + 128 * _c:
                                        2048 * _p + 128 * _c + 128],
                                    srt[:, 8 * _c:8 * _c + 8],
                                    start=(_c == 0), stop=False), waits=wts)
                    B.emit("tensor", lambda e, _c=ch, _p=par, _a=aux0:
                           e.matmul(P_x[:, _a + 8:_a + 136],
                                    x2T[:, 2048 * _p + 128 * _c:
                                        2048 * _p + 128 * _c + 128],
                                    sat[:, 128 * _c:128 * _c + 128],
                                    start=False, stop=(_c == 15)),
                           inc=(c["pe"] if ch == 15 else None))
                aux_done = c["pe"].v
                if t + 1 < NT:
                    xl2_next = plan_ln2(t + 1)
                gw_base = B.gw([0, 1, 2, 3])
                for ch in range(NCH):
                    for f in range(4):
                        B.emit("tensor", lambda e, _c=ch, _p=par, _f=f:
                               e.matmul(P_b[:, 512 * _f:512 * _f + 512],
                                        x2T[:, 2048 * _p + 128 * _c:
                                            2048 * _p + 128 * _c + 128],
                                        A2[:, 2048 * _c + 512 * _f:
                                            2048 * _c + 512 * _f + 512],
                                        start=(_c == 0), stop=False),
                               waits=(gw_base if (ch == 0 and f == 0) else ()))

                fv = B.emit("vector", lambda e, _a=aux0: e.max(
                    stop8[:], P_x[:, _a:_a + 8]),
                    waits=[(c["pe"], aux_done)], inc=c["dve"])
                fv = B.emit("vector", lambda e: e.tensor_scalar(
                    snm[:, 0:1], stop8[:, 0:1], -1.0, None, ALU.mult),
                    waits=[(c["dve"], fv)], inc=c["dve"])
                fv = B.emit("vector", lambda e, _a=aux0: e.tensor_scalar(
                    ssel[:], P_x[:, _a:_a + 8], stop8[:, 1:2], None,
                    ALU.is_ge), waits=[(c["dve"], fv)], inc=c["dve"])
                e8v = B.emit("scalar", lambda e, _a=aux0: e.activation(
                    se8[:], P_x[:, _a:_a + 8], AF.Exp, bias=snm[:, 0:1],
                    scale=1.0), waits=[(c["dve"], fv)], inc=c["act"])
                fv = B.emit("vector", lambda e: e.tensor_tensor(
                    sw8[:], se8[:], ssel[:], ALU.mult),
                    waits=[(c["act"], e8v), (c["dve"], fv)], inc=c["dve"])
                fv = B.emit("vector", lambda e: e.reduce_sum(
                    snm[:, 1:2], sw8[:], axis=AX.X),
                    waits=[(c["dve"], fv)], inc=c["dve"])
                fv = B.emit("vector", lambda e: e.reciprocal(
                    snm[:, 0:1], snm[:, 1:2]),
                    waits=[(c["dve"], fv)], inc=c["dve"])
                fv = B.emit("vector", lambda e: e.tensor_scalar(
                    sw8[:], sw8[:], snm[:, 0:1], 2.0, ALU.mult, ALU.mult),
                    waits=[(c["dve"], fv)], inc=c["dve"])
                uav = None
                for ex in range(8):
                    uav = B.emit("scalar", lambda e, _a=aux0, _x=ex:
                                 e.activation(
                                     su[:, 16 * _x:16 * _x + 16],
                                     P_x[:, _a + 8 + 16 * _x:
                                         _a + 8 + 16 * _x + 16],
                                     AF.Identity, scale=sw8[:, _x:_x + 1]),
                                 waits=([(c["dve"], fv)] if ex == 0 else ()),
                                 inc=(c["act"] if ex == 7 else None))
                uv = c["act"].v
                B.sg([auxbank], c["act"], uv)
                if t + 1 < NT:
                    plan_T(t + 1, xl2_next)
                obank = 4 + (1 - par)
                put = P_x[:, 512 * (1 - par):512 * (1 - par) + 64].bitcast(BF16)
                tv = B.emit("tensor", lambda e, _p=put: e.transpose(
                    _p[:, 0:128], su[:], sid),
                    waits=[(c["act"], uv)] + B.gw([obank]), inc=c["pe"])
                utv = B.emit("vector", lambda e, _p=put: e.tensor_copy(
                    sut[:], _p[:, 0:128]),
                    waits=[(c["pe"], tv)], inc=c["dve"])
                B.sg([obank], c["dve"], utv)
                for f in range(4):
                    B.emit("tensor", lambda e, _f=f: e.matmul(
                        P_b[:, 512 * _f:512 * _f + 512], sut[:],
                        sbf[:, 512 * _f:512 * _f + 512],
                        start=False, stop=True),
                        waits=([(c["dve"], utv)] if f == 0 else ()),
                        inc=(c["pe"] if f == 3 else None))
                base_done = c["pe"].v
                for f in range(4):
                    eng = "vector" if f < 2 else "scalar"
                    fn = ((lambda e, _f=f, _p=par: e.tensor_copy(
                        sff[:, 2048 * _p + 512 * _f:
                            2048 * _p + 512 * _f + 512],
                        P_b[:, 512 * _f:512 * _f + 512]))
                        if f < 2 else
                        (lambda e, _f=f, _p=par: e.activation(
                            sff[:, 2048 * _p + 512 * _f:
                                2048 * _p + 512 * _f + 512],
                            P_b[:, 512 * _f:512 * _f + 512], AF.Copy)))
                    B.emit(eng, fn, waits=[(c["pe"], base_done)],
                           inc=(c["dve"] if f == 1 else
                                (c["act"] if f == 3 else None)))
                ffv = c["dve"].v
                ffa = c["act"].v
                B.sg([0, 1], c["dve"], ffv)
                B.sg([2, 3], c["act"], ffa)
                rstd3, negmr3, dve_ln3 = plan_ln(
                    (lambda j, w, _p=par: sff[:, 2048 * _p + j:
                                              2048 * _p + j + w]),
                    2 + par, pre_waits=[(c["dve"], ffv), (c["act"], ffa)],
                    sq_scratch=sqs[:].bitcast(BF16))
                xsv = B.emit("scalar", lambda e, _t=t, _m=negmr3:
                             e.activation(x2s[:],
                                          A1[:, 2048 * _t:2048 * _t + 2048],
                                          AF.Identity, bias=_m, scale=1.0),
                             waits=[(c["dve"], dve_ln3)] + B.gw(["x2s"]),
                             inc=c["act"])
                ov = B.emit("vector", lambda e, _p=par, _r=rstd3:
                            e.scalar_tensor_tensor(
                                ost[:, 2048 * _p:2048 * _p + 2048],
                                sff[:, 2048 * _p:2048 * _p + 2048], _r,
                                x2s[:], ALU.mult, ALU.add),
                            waits=[(c["act"], xsv)] + B.gw([("ost", par)]),
                            inc=c["dve"])
                B.sg(["x2s"], c["dve"], ov)
                oc = c["out0"] if par == 0 else c["out1"]
                odv = B.emit("gpsimd", lambda e, _t=t, _p=par: e.dma_start(
                    out=out[128 * _t:128 * _t + 128, :],
                    in_=ost[:, 2048 * _p:2048 * _p + 2048]),
                    waits=[(c["dve"], ov)], inc=oc, inck=16)
                B.sg([("ost", par)], oc, odv)

            if upto == "ffn":
                dump_and_finish([
                    (sff[:].bitcast(BF16),),
                    (xln2[:],),
                    (x2T[:],),
                ])
            B.emit("gpsimd", lambda e: e.nop(),
                   waits=[(c["out0"], c["out0"].v),
                          (c["out1"], c["out1"].v)])

        with nc.Block() as block:
            B.run(block)
    return nc


_NC_CACHE = {}


def _get_program():
    if "p" not in _NC_CACHE:
        _NC_CACHE["p"] = build_program()
    return _NC_CACHE["p"]


def kernel(L0_attn_x, L0_attn_w_qkv, L0_attn_w_out, L0_ffn_weight,
           L0_ffn_lora_A, L0_ffn_lora_B, L0_ffn_router_w):
    x = np.asarray(L0_attn_x, dtype=np.float32)
    wqkv_h = np.ascontiguousarray(
        np.asarray(L0_attn_w_qkv, np.float32).reshape(D, 3, H, DH)
        .transpose(2, 0, 1, 3).reshape(H, D, 384))
    w_out_a = np.ascontiguousarray(np.asarray(L0_attn_w_out, np.float32))
    weightT_a = np.ascontiguousarray(np.asarray(L0_ffn_weight, np.float32).T)
    routerT_a = np.ascontiguousarray(np.asarray(L0_ffn_router_w, np.float32).T)
    at_a = np.ascontiguousarray(
        np.asarray(L0_ffn_lora_A, np.float32).reshape(128, D).T)
    bflat_a = np.ascontiguousarray(
        np.asarray(L0_ffn_lora_B, np.float32).transpose(0, 2, 1).reshape(128, D))

    nc = _get_program()
    in_maps = []
    for cix in range(8):
        in_maps.append({
            "x": np.ascontiguousarray(x[cix]),
            "wqkv": wqkv_h, "w_out": w_out_a, "weightT": weightT_a,
            "routerT": routerT_a, "at": at_a, "bflat": bflat_a,
        })
    res = run_bass_kernel_spmd(nc, in_maps, core_ids=list(range(8)))
    return np.stack([res.results[i]["out"] for i in range(8)], axis=0)


# version 16
# speedup vs baseline: 1.0998x; 1.0998x over previous
import numpy as np
import ml_dtypes
import concourse.bass as bass
import concourse.mybir as mybir
from concourse.bass_utils import run_bass_kernel_spmd
from contextlib import ExitStack

F32 = mybir.dt.float32
BF16 = mybir.dt.bfloat16
AF = mybir.ActivationFunctionType
ALU = mybir.AluOpType
AX = mybir.AxisListType

S = 2048
D = 2048
H = 16
DH = 128
NT = 16
NCH = 16
EXP_BIAS = -20.0
EPS = 1e-5
QSCALE = float(1.0 / np.sqrt(128.0))

ENGS = ("sync", "gpsimd", "vector", "scalar", "tensor")


class Ctr:
    def __init__(self, nc, name):
        self.name = name
        self.h = nc.alloc_semaphore(name)
        self.v = 0

    def inc(self, k=1):
        self.v += k
        return self.v


class Builder:
    def __init__(self, nc):
        self.nc = nc
        self.ops = {e: [] for e in ENGS}
        self.waited = {e: {} for e in ENGS}
        self.c = {n: Ctr(nc, n) for n in
                  ("ld_c", "ld_x0", "ld_x1", "ld_w0", "ld_w1", "ld_wo", "ld_wt",
                   "ldg", "out0", "out1", "pe", "act", "dve", "pool")}
        self.guard = {}

    def emit(self, eng, fn, waits=(), inc=None, inck=1):
        waits = [(cc, v) for (cc, v) in waits if cc is not None and v > 0]
        dedup = self.waited[eng]
        real = []
        for (cc, v) in waits:
            if dedup.get(cc.name, 0) < v:
                dedup[cc.name] = v
                real.append((cc.h, v))
        inch = None
        if inc is not None:
            inc.inc(inck)
            inch = inc.h

        def thunk(e, _w=tuple(real), _inch=inch, _fn=fn, _k=inck):
            for (h, v) in _w:
                e.wait_ge(h, v)
            ins = _fn(e)
            if _inch is not None:
                ins.then_inc(_inch, _k)
        self.ops[eng].append(thunk)
        return inc.v if inc is not None else None

    def gw(self, keys):
        ws = {}
        for k in keys:
            g = self.guard.get(k)
            if g is not None:
                cc, v = g
                if ws.get(cc.name, (cc, 0))[1] < v:
                    ws[cc.name] = (cc, v)
        return list(ws.values())

    def sg(self, keys, ctr, value):
        for k in keys:
            self.guard[k] = (ctr, value)

    def run(self, block):
        for eng in ENGS:
            lst = self.ops[eng]

            def body(e, _lst=lst):
                for fn in _lst:
                    fn(e)
            getattr(block, eng)(body)


def build_program(upto="full"):
    nc = bass.Bass("TRN2", target_bir_lowering=False,
                   dynamic_dma_scratch_size=16384)

    x_in = nc.dram_tensor("x", [S, D], F32, kind="ExternalInput").ap()
    wqkv = nc.dram_tensor("wqkv", [H, D, 384], F32, kind="ExternalInput").ap()
    w_out = nc.dram_tensor("w_out", [D, D], F32, kind="ExternalInput").ap()
    weightT = nc.dram_tensor("weightT", [D, D], F32, kind="ExternalInput").ap()
    routerT = nc.dram_tensor("routerT", [D, 8], F32, kind="ExternalInput").ap()
    at_in = nc.dram_tensor("at", [D, 128], F32, kind="ExternalInput").ap()
    bflat_in = nc.dram_tensor("bflat", [128, D], F32, kind="ExternalInput").ap()
    out = nc.dram_tensor("out", [S, D], F32, kind="ExternalOutput").ap()
    dbg = None
    if upto != "full":
        dbg = nc.dram_tensor("dbg", [4, 128, 32768], BF16,
                             kind="ExternalOutput").ap()

    cvals = np.zeros((128, 8), np.float32)
    cvals[:, 0] = EXP_BIAS
    cvals[:, 1] = EPS
    cpack = np.concatenate([
        np.eye(128, dtype=ml_dtypes.bfloat16).view(np.uint8),
        np.triu(np.ones((128, 128))).astype(ml_dtypes.bfloat16).view(np.uint8),
        cvals.view(np.uint8)], axis=1)
    consts_t = nc.inline_tensor(np.ascontiguousarray(cpack), "consts")

    with ExitStack() as st0:
        en = st0.enter_context
        A1 = en(nc.sbuf_tensor("A1", [128, 32768], BF16))
        A2 = en(nc.sbuf_tensor("A2", [128, 32768], BF16))
        cbuf = en(nc.sbuf_tensor("cbuf", [128, 544], mybir.dt.uint8))
        sid = cbuf[:, 0:256].bitcast(BF16)
        stri = cbuf[:, 256:512].bitcast(BF16)
        scon = cbuf[:, 512:544].bitcast(F32)
        sst = en(nc.sbuf_tensor("sst", [128, 4 * 40], F32))
        srcp = en(nc.sbuf_tensor("srcp", [128, 4], F32))

        B = Builder(nc)
        c = B.c
        EBIAS = scon[:, 0:1]
        EPSC = scon[:, 1:2]

        B.emit("sync", lambda s: s.dma_start(out=cbuf[:], in_=consts_t.ap()),
               inc=c["ld_c"], inck=16)
        consts_c = (c["ld_c"], 16)

        def stc(par, i0, i1=None):
            base = 40 * par
            if i1 is None:
                i1 = i0 + 1
            return sst[:, base + i0: base + i1]

        def plan_ln(src_fn, par, pre_waits, sq_scratch=None,
                    psum_src=False):
            base = 40 * par
            full = src_fn(0, 2048) if True else None
            fl = B.emit("vector", lambda e, _f=full, _o=stc(par, 32):
                        e.reduce_sum(_o, _f, axis=AX.X),
                        waits=list(pre_waits) + B.gw([("sst", par)]),
                        inc=c["dve"])
            sq = B.emit("scalar", lambda e, _f=full, _s=sq_scratch,
                        _o=stc(par, 34): e.activation(
                            _s, _f, AF.Square, accum_out=_o),
                        waits=list(pre_waits) + B.gw([("sstA", par)]) +
                        ([(c["dve"], fl)] if psum_src else []),
                        inc=c["act"])
            fl = B.emit("vector", lambda e, _i=stc(par, 32), _o=stc(par, 35):
                        e.tensor_scalar(_o, _i, 1.0 / 2048.0, None, ALU.mult),
                        waits=[(c["dve"], fl), (c["act"], sq)], inc=c["dve"])
            fl = B.emit("vector", lambda e, _i=stc(par, 34), _o=stc(par, 39):
                        e.tensor_scalar(_o, _i, 1.0 / 2048.0, None, ALU.mult),
                        waits=[(c["dve"], fl)], inc=c["dve"])
            fl = B.emit("vector", lambda e, _m=stc(par, 35), _t=stc(par, 39),
                        _o=stc(par, 36):
                        e.scalar_tensor_tensor(_o, _m, stc(par, 35), _t,
                                               ALU.mult, ALU.subtract),
                        waits=[(c["dve"], fl)], inc=c["dve"])
            lnv = B.emit("scalar", lambda e, _i=stc(par, 36), _o=stc(par, 39):
                         e.activation(_o, _i, AF.Ln, bias=EPSC, scale=-1.0),
                         waits=[(c["dve"], fl), consts_c], inc=c["act"])
            av = B.emit("scalar", lambda e, _i=stc(par, 39), _o=stc(par, 37):
                        e.activation(_o, _i, AF.Exp, scale=-0.5),
                        waits=[(c["act"], lnv)], inc=c["act"])
            dv = B.emit("vector", lambda e, _m=stc(par, 35), _o=stc(par, 38):
                        e.tensor_scalar(_o, _m, stc(par, 37), -1.0,
                                        ALU.mult, ALU.mult),
                        waits=[(c["act"], av)], inc=c["dve"])
            B.sg([("sst", par)], c["dve"], dv)
            B.sg([("sstA", par)], c["act"], av)
            return stc(par, 37), stc(par, 38), dv

        with ExitStack() as ph:
            pen = ph.enter_context
            xln = pen(nc.sbuf_tensor("xln", [128, 2 * 2048], BF16))
            wbf = pen(nc.sbuf_tensor("wbf", [128, 2 * 6144], BF16))
            Qbf = pen(nc.sbuf_tensor("Qbf", [128, 2 * 2048], BF16))
            Kbf = pen(nc.sbuf_tensor("Kbf", [128, 2 * 2048], BF16))
            VTs = pen(nc.sbuf_tensor("VTs", [128, 2 * 2048], BF16))
            att = pen(nc.sbuf_tensor("att", [128, 4 * 128], BF16))
            Vaug = pen(nc.sbuf_tensor("Vaug", [128, 16 * 129], BF16))
            expT = pen(nc.sbuf_tensor("expT", [128, 16 * 512], BF16))
            xio2slot = expT[:, 0:8192].bitcast(F32)
            P_qkv = pen(nc.psum_tensor("P_qkv", [128, 2048], F32))
            P_sc = pen(nc.psum_tensor("P_sc", [128, 1024], F32))
            P_t1 = pen(nc.psum_tensor("P_t1", [128, 1024], BF16))
            P_t2 = pen(nc.psum_tensor("P_t2", [128, 1024], BF16))

            ln1_slots = [P_qkv[:, 0:512].bitcast(BF16),
                         P_qkv[:, 512:1024].bitcast(BF16),
                         P_qkv[:, 1024:1536].bitcast(BF16),
                         P_qkv[:, 1536:2048].bitcast(BF16),
                         P_sc[:, 0:512].bitcast(BF16),
                         P_sc[:, 512:1024].bitcast(BF16),
                         P_t1, P_t2]
            for t in range(NT):
                par = t % 4
                bpar = t % 2
                xio = xio2slot[:, 2048 * bpar:2048 * bpar + 2048]
                xsem = c["ld_x0"] if bpar == 0 else c["ld_x1"]
                xrdy = B.emit("sync", lambda s, _t=t, _x=xio: s.dma_start(
                    out=_x, in_=x_in[128 * _t:128 * _t + 128, :]),
                    waits=B.gw([("xio", bpar)]), inc=xsem, inck=16)
                rstd, negmr, dve_ln = plan_ln(
                    (lambda j, w, _x=xio: _x[:, j:j + w]), par,
                    pre_waits=[(xsem, xrdy)],
                    sq_scratch=Vaug[:, 0:2048])
                xlv = B.emit("scalar", lambda e, _p=bpar, _r=rstd, _m=negmr,
                             _x=xio: e.activation(
                                 xln[:, 2048 * _p:2048 * _p + 2048],
                                 _x, AF.Identity, bias=_m, scale=_r),
                             waits=[(c["dve"], dve_ln)] + B.gw([("xln", bpar)]),
                             inc=c["act"])
                B.sg([("xio", bpar)], c["act"], xlv)
                for grp in range(4):
                    bank = (4 * t + grp) % 8
                    slot = ln1_slots[bank]
                    gwb = B.gw([bank])
                    for j in range(4):
                        ch = 4 * grp + j
                        B.emit("tensor", lambda e, _p=bpar, _c=ch, _j=j,
                               _pt=slot: e.transpose(
                                   _pt[:, 128 * _j:128 * _j + 128],
                                   xln[:, 2048 * _p + 128 * _c:
                                       2048 * _p + 128 * _c + 128], sid),
                               waits=([(c["act"], xlv), consts_c] + gwb)
                               if j == 0 else (),
                               inc=(c["pe"] if j == 3 else None))
                    tv = c["pe"].v
                    dv = B.emit("vector", lambda e, _t=t, _g=grp, _pt=slot:
                                e.tensor_copy(
                                    A1[:].rearrange("p (c f) -> p c f",
                                                    f=2048)
                                    [:, 4 * _g:4 * _g + 4,
                                     128 * _t:128 * _t + 128],
                                    _pt[:, 0:512].rearrange(
                                        "p (c f) -> p c f", f=128)),
                                waits=[(c["pe"], tv)], inc=c["dve"])
                    B.sg([bank], c["dve"], dv)
                B.sg([("xln", bpar)], c["pe"], c["pe"].v)
            xlnT_dve = c["dve"].v
            marks = {"ln1_pe": c["pe"].v}
            vaug_ones = B.emit("vector", lambda e: e.memset(Vaug[:], 1.0),
                               waits=[(c["act"], c["act"].v)], inc=c["dve"])
            if upto == "ln1":
                dump_and_finish([(A1[:],)])
                with nc.Block() as block:
                    B.run(block)
                return nc

            vt_blocks = [[0] * 4, [0] * 4]

            def plan_wdma(hh):
                wpar = hh % 2
                wsem = c["ld_w0"] if wpar == 0 else c["ld_w1"]
                wrdy = B.emit("gpsimd", lambda s, _h=hh, _p=wpar: s.dma_start(
                    out=wbf[:, 6144 * _p:6144 * _p + 6144]
                    .rearrange("p (x f) -> p x f", f=384),
                    in_=wqkv[_h].rearrange("(x p) f -> p x f", p=128)),
                    waits=B.gw([("wbf", wpar)]), inc=wsem, inck=16)
                return wsem, wrdy

            def make_qkv_units(hh, wsem, wrdy):
                wpar = hh % 2
                units = []
                gidx = [0]

                def mk(m, half, tb, first):
                    col0 = 1024 * half + 512 * tb

                    def unit():
                        bank = gidx[0] % 2
                        gidx[0] += 1
                        gw = B.gw([bank])
                        for ch in range(NCH):
                            wts = ()
                            if ch == 0:
                                wts = gw + ([(wsem, wrdy),
                                             (c["dve"], xlnT_dve)]
                                            if first else [])
                            B.emit("tensor", lambda e, _c=ch, _m=m, _q=col0,
                                   _p=wpar, _b=bank: e.matmul(
                                       P_qkv[:, 512 * _b:512 * _b + 512],
                                       wbf[:, 6144 * _p + 384 * _c
                                           + 128 * _m:
                                           6144 * _p + 384 * _c
                                           + 128 * _m + 128],
                                       A1[:, 2048 * _c + _q:
                                           2048 * _c + _q + 512],
                                       start=(_c == 0), stop=(_c == 15)),
                                   waits=wts,
                                   inc=(c["pe"] if ch == 15 else None))
                        mmv = c["pe"].v
                        dst = (Qbf, Kbf, VTs)[m]
                        if m == 0:
                            ev = B.emit("vector", lambda e, _q=col0, _b=bank,
                                        _hb=wpar: e.tensor_scalar(
                                            Qbf[:, 2048 * _hb + _q:
                                                2048 * _hb + _q + 512],
                                            P_qkv[:, 512 * _b:512 * _b + 512],
                                            QSCALE, None, ALU.mult),
                                        waits=[(c["pe"], mmv)], inc=c["dve"])
                        else:
                            ev = B.emit("vector", lambda e, _q=col0, _d=dst,
                                        _b=bank, _hb=wpar: e.tensor_copy(
                                            _d[:, 2048 * _hb + _q:
                                                2048 * _hb + _q + 512],
                                            P_qkv[:, 512 * _b:512 * _b + 512]),
                                        waits=[(c["pe"], mmv)], inc=c["dve"])
                        B.sg([bank], c["dve"], ev)
                        if m == 2:
                            vt_blocks[wpar][2 * half + tb] = ev
                        if m == 2 and half == 1 and tb == 1:
                            B.sg([("wbf", wpar)], c["pe"], c["pe"].v)
                    return unit

                first = True
                for m in range(3):
                    for half in range(2):
                        for tb in range(2):
                            units.append(mk(m, half, tb, first))
                            first = False
                return units

            wsem0, wrdy0 = plan_wdma(0)
            for u in make_qkv_units(0, wsem0, wrdy0):
                u()

            for h in range(H):
                hb = h % 2
                if h == H - 1:
                    marks["wbf_pe"] = c["pe"].v
                for grp in range(2):
                    pt = P_t1 if grp == 0 else P_t2
                    bank = 6 + grp
                    gwb = B.gw([bank])
                    for j in range(8):
                        k = 8 * grp + j
                        B.emit("tensor", lambda e, _k=k, _j=j, _pt=pt,
                               _hb=hb: e.transpose(
                                   _pt[:, 128 * _j:128 * _j + 128],
                                   VTs[:, 2048 * _hb + 128 * _k:
                                       2048 * _hb + 128 * _k + 128], sid),
                               waits=([(c["dve"],
                                        max(vt_blocks[hb][2 * grp],
                                            vt_blocks[hb][2 * grp + 1]))]
                                      + gwb
                                      + ([(c["dve"], vaug_ones)]
                                         if h == 0 and grp == 0 else []))
                               if j == 0 else (),
                               inc=(c["pe"] if j == 7 else None))
                    tv = c["pe"].v
                    av = B.emit("vector", lambda e, _g=grp, _pt=pt:
                                e.tensor_copy(
                                    Vaug[:].rearrange("p (k f) -> p k f",
                                                      f=129)
                                    [:, 8 * _g:8 * _g + 8, 0:128],
                                    _pt[:, 0:1024].rearrange(
                                        "p (k f) -> p k f", f=128)),
                                waits=[(c["pe"], tv)], inc=c["dve"])
                    B.sg([bank], c["dve"], av)
                vaug_dve = c["dve"].v

                if h == 0:
                    pend_T = []

                    def drain_T(item):
                        _h, _qt, _p, _attv = item
                        pt2 = P_t1 if _qt % 2 == 0 else P_t2
                        pbank2 = 6 + _qt % 2
                        tv2 = B.emit("tensor", lambda e, _pt=pt2, _pp=_p:
                                     e.transpose(_pt[:, 0:128],
                                                 att[:, 128 * _pp:
                                                     128 * _pp + 128], sid),
                                     waits=[(c["dve"], _attv)]
                                     + B.gw([pbank2]), inc=c["pe"])
                        B.sg([("att", _p)], c["pe"], tv2)
                        av2 = B.emit("vector", lambda e, _hh=_h, _q2=_qt,
                                     _pt=pt2: e.tensor_copy(
                                         A2[:, 2048 * _hh + 128 * _q2:
                                             2048 * _hh + 128 * _q2 + 128],
                                         _pt[:, 0:128]),
                                     waits=[(c["pe"], tv2)], inc=c["dve"])
                        B.sg([pbank2], c["dve"], av2)

                if h + 1 < H:
                    wsemN, wrdyN = plan_wdma(h + 1)
                    units = make_qkv_units(h + 1, wsemN, wrdyN)
                else:
                    units = []
                ui = [0]

                def pump(n):
                    for _ in range(n):
                        if ui[0] < len(units):
                            units[ui[0]]()
                            ui[0] += 1

                for qb in range(4):
                    exp_done = {}
                    mask_done = {}
                    for i0 in range(0, 4 * qb, 2):
                        gwp = B.gw([4, 5])
                        for d in range(2):
                            B.emit("tensor", lambda e, _i=i0 + d, _qb=qb,
                                   _d=d, _hb=hb: e.matmul(
                                       P_sc[:, 512 * _d:512 * _d + 512],
                                       Kbf[:, 2048 * _hb + 128 * _i:
                                           2048 * _hb + 128 * _i + 128],
                                       Qbf[:, 2048 * _hb + 512 * _qb:
                                           2048 * _hb + 512 * _qb + 512],
                                       start=True, stop=True),
                                   waits=(gwp + [(c["dve"], vaug_dve)])
                                   if d == 0 else (),
                                   inc=(c["pe"] if d == 1 else None))
                        scv = c["pe"].v
                        ev = B.emit("scalar", lambda e, _i=i0:
                                    e.activation(
                                        expT[:, 512 * _i:512 * _i + 1024],
                                        P_sc[:, 0:1024], AF.Exp,
                                        bias=EBIAS, scale=1.0),
                                    waits=[(c["pe"], scv)], inc=c["act"])
                        exp_done[i0] = ev
                        exp_done[i0 + 1] = ev
                        B.sg([4, 5], c["act"], ev)
                        pump(1)
                    for s in range(4):
                        i = 4 * qb + s
                        sub = s % 2
                        bank = 4 + sub
                        n = 512 - 128 * s
                        scv = B.emit("tensor", lambda e, _i=i, _qb=qb,
                                     _u=sub, _o=128 * s, _hb=hb:
                                     e.matmul(
                                         P_sc[:, 512 * _u + _o:
                                             512 * _u + 512],
                                         Kbf[:, 2048 * _hb + 128 * _i:
                                             2048 * _hb + 128 * _i + 128],
                                         Qbf[:, 2048 * _hb + 512 * _qb + _o:
                                             2048 * _hb + 512 * _qb + 512],
                                         start=True, stop=True),
                                     waits=B.gw([bank]) +
                                     [(c["dve"], vaug_dve)], inc=c["pe"])
                        ev = B.emit("scalar", lambda e, _i=i, _u=sub,
                                    _o=128 * s: e.activation(
                                        expT[:, 512 * _i + _o:
                                            512 * _i + 512],
                                        P_sc[:, 512 * _u + _o:
                                            512 * _u + 512],
                                        AF.Exp, bias=EBIAS, scale=1.0),
                                    waits=[(c["pe"], scv)], inc=c["act"])
                        exp_done[i] = ev
                        B.sg([bank], c["act"], ev)
                    if h == H - 1 and qb == 3:
                        marks["qk_pe"] = c["pe"].v
                    for w in range(2):
                        base = 2048 * qb + 1280 * w
                        ones_ap = expT[:, base:base + 768]
                        dv3 = bass.AP(ones_ap.tensor, ones_ap.offset,
                                      [list(p) for p in ones_ap.ap[:1]] +
                                      [[640, 2], [1, 128]])
                        pv = B.emit("vector", lambda e, _v=dv3:
                                    e.tensor_tensor(
                                        _v, _v,
                                        stri[:].unsqueeze(1)
                                        .broadcast_to([128, 2, 128]),
                                        ALU.mult),
                                    waits=[(c["act"],
                                            exp_done[4 * qb + 2 * w + 1])],
                                    inc=c["dve"])
                        mask_done[4 * qb + 2 * w] = pv
                        mask_done[4 * qb + 2 * w + 1] = pv
                    for u2 in range(4):
                        qt = 4 * qb + u2
                        avsl = 1024 + 512 * (qt % 2)
                        bank = 2 + qt % 2
                        wts = B.gw([bank]) + [(c["act"], exp_done[qt]),
                                              (c["dve"], vaug_dve)]
                        if qt in mask_done:
                            wts.append((c["dve"], mask_done[qt]))
                        for i in range(qt + 1):
                            B.emit("tensor", lambda e, _i=i, _u=u2, _a=avsl,
                                   _q=qt: e.matmul(
                                       P_qkv[:, _a:_a + 129],
                                       expT[:, 512 * _i + 128 * _u:
                                           512 * _i + 128 * _u + 128],
                                       Vaug[:, 129 * _i:129 * _i + 129],
                                       start=(_i == 0), stop=(_i == _q)),
                                   waits=(wts if i == 0 else ()),
                                   inc=(c["pe"] if i == qt else None))
                        avv = c["pe"].v
                        apar = qt % 4
                        fv = B.emit("vector", lambda e, _a=avsl, _p=apar:
                                    e.reciprocal(
                                        srcp[:, _p:_p + 1],
                                        P_qkv[:, _a + 128:_a + 129]),
                                    waits=[(c["pe"], avv)]
                                    + B.gw([("rcp", apar)]), inc=c["dve"])
                        attv = B.emit("vector", lambda e, _a=avsl, _p=apar:
                                      e.tensor_scalar(
                                          att[:, 128 * _p:128 * _p + 128],
                                          P_qkv[:, _a:_a + 128],
                                          srcp[:, _p:_p + 1], None,
                                          ALU.mult),
                                      waits=[(c["dve"], fv)]
                                      + B.gw([("att", apar)]),
                                      inc=c["dve"])
                        B.sg([("rcp", apar)], c["dve"], attv)
                        B.sg([bank], c["dve"], attv)
                        pend_T.append((h, qt, apar, attv))
                        if len(pend_T) > 3:
                            drain_T(pend_T.pop(0))
                pump(12)
            attn_act = c["act"].v
            while pend_T:
                drain_T(pend_T.pop(0))
            attn_act = c["act"].v
            attn_pe = c["pe"].v
            attn_dve = c["dve"].v
            attn_pool = c["pool"].v
            if upto == "qkv1":
                dump_and_finish([(Qbf[:],), (Kbf[:],), (Vaug[:].rearrange("p x -> p x") if False else expT[:],)])
                with nc.Block() as block:
                    B.run(block)
                return nc
            if upto == "attn":
                dump_and_finish([(A2[:],)])
                with nc.Block() as block:
                    B.run(block)
                return nc

        with ExitStack() as ph:
            pen = ph.enter_context
            wo_bf = pen(nc.sbuf_tensor("wo_bf", [128, 32768], BF16))
            xio2 = pen(nc.sbuf_tensor("xio2", [128, 2048], F32))
            ntmp = pen(nc.sbuf_tensor("ntmp", [128, 2048], BF16))
            P_e = pen(nc.psum_tensor("P_e", [128, 2048], F32))
            P_o = pen(nc.psum_tensor("P_o", [128, 2048], F32))

            scope_w = [(c["pe"], attn_pe), (c["act"], attn_act),
                       (c["dve"], attn_dve), (c["pool"], attn_pool)]
            wo_pieces = [(0, 2, [(c["pe"], marks["ln1_pe"])]),
                         (2, 8, [(c["pe"], marks["wbf_pe"])]),
                         (8, 14, [(c["pe"], marks["qk_pe"])]),
                         (14, 16, scope_w)]
            for c0, c1, wwts in wo_pieces:
                B.emit("gpsimd", lambda s, _a=c0, _b=c1: s.dma_start(
                    out=wo_bf[:, 2048 * _a:2048 * _b]
                    .rearrange("p (x f) -> p x f", f=2048),
                    in_=w_out[128 * _a:128 * _b, :]
                    .rearrange("(x p) f -> p x f", p=128)),
                    waits=wwts, inc=c["ld_wo"], inck=16)
            wo_all = c["ld_wo"].v

            for t in range(NT):
                par = t % 2
                P = P_e if par == 0 else P_o
                banks = [0, 1, 2, 3] if par == 0 else [4, 5, 6, 7]
                gww = B.gw(banks)
                xrdy = B.emit("gpsimd", lambda e, _t=t: e.dma_start(
                    out=xio2[:], in_=x_in[128 * _t:128 * _t + 128, :]),
                    waits=B.gw(["xio2"]), inc=c["ldg"], inck=16)
                for ch in range(NCH):
                    wts = ()
                    if ch == 0:
                        wts = gww + [(c["ld_wo"], wo_all),
                                     (c["act"], attn_act)]
                    for f in range(4):
                        B.emit("tensor", lambda e, _c=ch, _t=t, _f=f, _P=P:
                               e.matmul(
                                   _P[:, 512 * _f:512 * _f + 512],
                                   A2[:, 2048 * _c + 128 * _t:
                                       2048 * _c + 128 * _t + 128],
                                   wo_bf[:, 2048 * _c + 512 * _f:
                                       2048 * _c + 512 * _f + 512],
                                   start=(_c == 0), stop=(_c == 15)),
                               waits=(wts if f == 0 else ()),
                               inc=(c["pe"] if (ch == 15 and f == 3) else None))
                mmv = c["pe"].v
                rstd, negmr, dve_ln = plan_ln(
                    (lambda j, w, _P=P: _P[:, j:j + w]), par,
                    pre_waits=[(c["pe"], mmv)] + B.gw(["ntmp"]),
                    sq_scratch=ntmp[:], psum_src=True)
                nv = B.emit("scalar", lambda e, _P=P, _r=rstd, _m=negmr:
                            e.activation(ntmp[:], _P[:], AF.Identity,
                                         bias=_m, scale=_r),
                            waits=[(c["dve"], dve_ln)] + B.gw(["ntmp"]),
                            inc=c["act"])
                x2v = B.emit("vector", lambda e, _t=t: e.tensor_tensor(
                    A1[:, 2048 * _t:2048 * _t + 2048], ntmp[:], xio2[:],
                    ALU.add),
                    waits=[(c["act"], nv), (c["ldg"], xrdy)], inc=c["dve"])
                B.sg(banks, c["dve"], x2v)
                B.sg(["xio2"], c["dve"], x2v)
                B.sg(["ntmp"], c["dve"], x2v)
                B.sg([("x2", t)], c["dve"], x2v)
            outproj_pe = c["pe"].v
            outproj_dve = c["dve"].v
            outproj_act = c["act"].v
            if upto == "outproj":
                dump_and_finish([(A1[:],), (A2[:],), (wo_bf[:],)])
                with nc.Block() as block:
                    B.run(block)
                return nc

        with ExitStack() as ph:
            pen = ph.enter_context
            xln2 = pen(nc.sbuf_tensor("xln2", [128, 2 * 2048], BF16))
            x2T = pen(nc.sbuf_tensor("x2T", [128, 2 * 2048], BF16))
            sbf = pen(nc.sbuf_tensor("sbf", [128, 2048], BF16))
            sat = pen(nc.sbuf_tensor("sat", [128, 2048], BF16))
            srt = pen(nc.sbuf_tensor("srt", [128, 128], BF16))
            sff = pen(nc.sbuf_tensor("sff", [128, 2 * 2048], F32))
            sqs = pen(nc.sbuf_tensor("sqs", [128, 1024], F32))
            x2s = pen(nc.sbuf_tensor("x2s", [128, 2048], F32))
            ost = pen(nc.sbuf_tensor("ost", [128, 2 * 2048], F32))
            stop8 = pen(nc.sbuf_tensor("stop8", [128, 8], F32))
            ssel = pen(nc.sbuf_tensor("ssel", [128, 8], F32))
            se8 = pen(nc.sbuf_tensor("se8", [128, 8], F32))
            sw8 = pen(nc.sbuf_tensor("sw8", [128, 8], F32))
            snm = pen(nc.sbuf_tensor("snm", [128, 2], F32))
            su = pen(nc.sbuf_tensor("su", [128, 128], BF16))
            sut = pen(nc.sbuf_tensor("sut", [128, 128], BF16))
            P_b = pen(nc.psum_tensor("P_b", [128, 2048], F32))
            P_x = pen(nc.psum_tensor("P_x", [128, 1024], F32))
            P_u1 = pen(nc.psum_tensor("P_u1", [128, 1024], BF16))
            P_u2 = pen(nc.psum_tensor("P_u2", [128, 1024], BF16))

            scope2 = [(c["pe"], outproj_pe), (c["dve"], outproj_dve),
                      (c["act"], outproj_act)]
            for hh in range(2):
                B.emit("gpsimd", lambda s, _h=hh: s.dma_start(
                    out=A2[:, 16384 * _h:16384 * _h + 16384]
                    .rearrange("p (x f) -> p x f", f=2048),
                    in_=weightT[1024 * _h:1024 * _h + 1024, :]
                    .rearrange("(x p) f -> p x f", p=128)),
                    waits=(scope2 if hh == 0 else []),
                    inc=c["ld_wt"], inck=16)
            B.emit("gpsimd", lambda s: s.dma_start(
                out=srt[:].rearrange("p (x e) -> p x e", e=8),
                in_=routerT.rearrange("(x p) e -> p x e", p=128)),
                inc=c["ld_wt"], inck=16)
            B.emit("gpsimd", lambda s: s.dma_start(
                out=sat[:].rearrange("p (x f) -> p x f", f=128),
                in_=at_in.rearrange("(x p) f -> p x f", p=128)),
                inc=c["ld_wt"], inck=16)
            B.emit("gpsimd", lambda s: s.dma_start(
                out=sbf[:], in_=bflat_in), inc=c["ld_wt"], inck=16)
            wT_all = c["ld_wt"].v

            def plan_ln2(t):
                par = t % 2
                rstd, negmr, dve_ln = plan_ln(
                    (lambda j, w, _t=t: A1[:, 2048 * _t + j:
                                           2048 * _t + j + w]),
                    par, pre_waits=B.gw([("x2", t)]),
                    sq_scratch=sqs[:].bitcast(BF16))
                return B.emit("scalar", lambda e, _p=par, _t=t, _r=rstd,
                              _m=negmr: e.activation(
                                  xln2[:, 2048 * _p:2048 * _p + 2048],
                                  A1[:, 2048 * _t:2048 * _t + 2048],
                                  AF.Identity, bias=_m, scale=_r),
                              waits=[(c["dve"], dve_ln)]
                              + B.gw([("xln2", par)]), inc=c["act"])

            xl2_next = plan_ln2(0)
            x2T_ready = [0] * NCH

            def plan_T(t, xl2):
                par = t % 2
                for grp in range(2):
                    pt = P_u1 if grp == 0 else P_u2
                    bank = 6 + grp
                    gwb = B.gw([bank])
                    for j in range(8):
                        ch = 8 * grp + j
                        B.emit("tensor", lambda e, _p=par, _c=ch, _j=j,
                               _pt=pt: e.transpose(
                                   _pt[:, 128 * _j:128 * _j + 128],
                                   xln2[:, 2048 * _p + 128 * _c:
                                       2048 * _p + 128 * _c + 128], sid),
                               waits=([(c["act"], xl2)] + gwb) if j == 0
                               else (), inc=(c["pe"] if j == 7 else None))
                    tv = c["pe"].v
                    av = B.emit("scalar", lambda e, _p=par, _g=grp, _pt=pt:
                                e.activation(
                                    x2T[:, 2048 * _p + 1024 * _g:
                                        2048 * _p + 1024 * _g + 1024],
                                    _pt[:, 0:1024], AF.Copy),
                                waits=[(c["pe"], tv)], inc=c["act"])
                    B.sg([bank], c["act"], av)
                    for j in range(8):
                        x2T_ready[8 * grp + j] = av
                B.sg([("xln2", par)], c["pe"], c["pe"].v)

            plan_T(0, xl2_next)
            for t in range(NT):
                par = t % 2
                aux0 = 512 * par
                auxbank = 4 + par
                gw_aux = B.gw([auxbank])
                for ch in range(NCH):
                    wts = [(c["act"], x2T_ready[ch])]
                    if ch == 0:
                        wts += gw_aux + [(c["ld_wt"], wT_all)]
                    B.emit("tensor", lambda e, _c=ch, _p=par, _a=aux0:
                           e.matmul(P_x[:, _a:_a + 8],
                                    x2T[:, 2048 * _p + 128 * _c:
                                        2048 * _p + 128 * _c + 128],
                                    srt[:, 8 * _c:8 * _c + 8],
                                    start=(_c == 0), stop=False), waits=wts)
                    B.emit("tensor", lambda e, _c=ch, _p=par, _a=aux0:
                           e.matmul(P_x[:, _a + 8:_a + 136],
                                    x2T[:, 2048 * _p + 128 * _c:
                                        2048 * _p + 128 * _c + 128],
                                    sat[:, 128 * _c:128 * _c + 128],
                                    start=False, stop=(_c == 15)),
                           inc=(c["pe"] if ch == 15 else None))
                aux_done = c["pe"].v
                if t + 1 < NT:
                    xl2_next = plan_ln2(t + 1)
                gw_base = B.gw([0, 1, 2, 3])
                for ch in range(NCH):
                    for f in range(4):
                        B.emit("tensor", lambda e, _c=ch, _p=par, _f=f:
                               e.matmul(P_b[:, 512 * _f:512 * _f + 512],
                                        x2T[:, 2048 * _p + 128 * _c:
                                            2048 * _p + 128 * _c + 128],
                                        A2[:, 2048 * _c + 512 * _f:
                                            2048 * _c + 512 * _f + 512],
                                        start=(_c == 0), stop=False),
                               waits=(gw_base if (ch == 0 and f == 0) else ()))

                fv = B.emit("vector", lambda e, _a=aux0: e.max(
                    stop8[:], P_x[:, _a:_a + 8]),
                    waits=[(c["pe"], aux_done)], inc=c["dve"])
                fv = B.emit("vector", lambda e: e.tensor_scalar(
                    snm[:, 0:1], stop8[:, 0:1], -1.0, None, ALU.mult),
                    waits=[(c["dve"], fv)], inc=c["dve"])
                fv = B.emit("vector", lambda e, _a=aux0: e.tensor_scalar(
                    ssel[:], P_x[:, _a:_a + 8], stop8[:, 1:2], None,
                    ALU.is_ge), waits=[(c["dve"], fv)], inc=c["dve"])
                e8v = B.emit("scalar", lambda e, _a=aux0: e.activation(
                    se8[:], P_x[:, _a:_a + 8], AF.Exp, bias=snm[:, 0:1],
                    scale=1.0), waits=[(c["dve"], fv)], inc=c["act"])
                fv = B.emit("vector", lambda e: e.tensor_tensor(
                    sw8[:], se8[:], ssel[:], ALU.mult),
                    waits=[(c["act"], e8v), (c["dve"], fv)], inc=c["dve"])
                fv = B.emit("vector", lambda e: e.reduce_sum(
                    snm[:, 1:2], sw8[:], axis=AX.X),
                    waits=[(c["dve"], fv)], inc=c["dve"])
                fv = B.emit("vector", lambda e: e.reciprocal(
                    snm[:, 0:1], snm[:, 1:2]),
                    waits=[(c["dve"], fv)], inc=c["dve"])
                fv = B.emit("vector", lambda e: e.tensor_scalar(
                    sw8[:], sw8[:], snm[:, 0:1], 2.0, ALU.mult, ALU.mult),
                    waits=[(c["dve"], fv)], inc=c["dve"])
                uav = None
                for ex in range(8):
                    uav = B.emit("scalar", lambda e, _a=aux0, _x=ex:
                                 e.activation(
                                     su[:, 16 * _x:16 * _x + 16],
                                     P_x[:, _a + 8 + 16 * _x:
                                         _a + 8 + 16 * _x + 16],
                                     AF.Identity, scale=sw8[:, _x:_x + 1]),
                                 waits=([(c["dve"], fv)] if ex == 0 else ()),
                                 inc=(c["act"] if ex == 7 else None))
                uv = c["act"].v
                B.sg([auxbank], c["act"], uv)
                if t + 1 < NT:
                    plan_T(t + 1, xl2_next)
                obank = 4 + (1 - par)
                put = P_x[:, 512 * (1 - par):512 * (1 - par) + 64].bitcast(BF16)
                tv = B.emit("tensor", lambda e, _p=put: e.transpose(
                    _p[:, 0:128], su[:], sid),
                    waits=[(c["act"], uv)] + B.gw([obank]), inc=c["pe"])
                utv = B.emit("vector", lambda e, _p=put: e.tensor_copy(
                    sut[:], _p[:, 0:128]),
                    waits=[(c["pe"], tv)], inc=c["dve"])
                B.sg([obank], c["dve"], utv)
                for f in range(4):
                    B.emit("tensor", lambda e, _f=f: e.matmul(
                        P_b[:, 512 * _f:512 * _f + 512], sut[:],
                        sbf[:, 512 * _f:512 * _f + 512],
                        start=False, stop=True),
                        waits=([(c["dve"], utv)] if f == 0 else ()),
                        inc=(c["pe"] if f == 3 else None))
                base_done = c["pe"].v
                for f in range(4):
                    eng = "vector" if f < 2 else "scalar"
                    fn = ((lambda e, _f=f, _p=par: e.tensor_copy(
                        sff[:, 2048 * _p + 512 * _f:
                            2048 * _p + 512 * _f + 512],
                        P_b[:, 512 * _f:512 * _f + 512]))
                        if f < 2 else
                        (lambda e, _f=f, _p=par: e.activation(
                            sff[:, 2048 * _p + 512 * _f:
                                2048 * _p + 512 * _f + 512],
                            P_b[:, 512 * _f:512 * _f + 512], AF.Copy)))
                    B.emit(eng, fn, waits=[(c["pe"], base_done)],
                           inc=(c["dve"] if f == 1 else
                                (c["act"] if f == 3 else None)))
                ffv = c["dve"].v
                ffa = c["act"].v
                B.sg([0, 1], c["dve"], ffv)
                B.sg([2, 3], c["act"], ffa)
                rstd3, negmr3, dve_ln3 = plan_ln(
                    (lambda j, w, _p=par: sff[:, 2048 * _p + j:
                                              2048 * _p + j + w]),
                    2 + par, pre_waits=[(c["dve"], ffv), (c["act"], ffa)],
                    sq_scratch=sqs[:].bitcast(BF16))
                xsv = B.emit("scalar", lambda e, _t=t, _m=negmr3:
                             e.activation(x2s[:],
                                          A1[:, 2048 * _t:2048 * _t + 2048],
                                          AF.Identity, bias=_m, scale=1.0),
                             waits=[(c["dve"], dve_ln3)] + B.gw(["x2s"]),
                             inc=c["act"])
                ov = B.emit("vector", lambda e, _p=par, _r=rstd3:
                            e.scalar_tensor_tensor(
                                ost[:, 2048 * _p:2048 * _p + 2048],
                                sff[:, 2048 * _p:2048 * _p + 2048], _r,
                                x2s[:], ALU.mult, ALU.add),
                            waits=[(c["act"], xsv)] + B.gw([("ost", par)]),
                            inc=c["dve"])
                B.sg(["x2s"], c["dve"], ov)
                oc = c["out0"] if par == 0 else c["out1"]
                odv = B.emit("gpsimd", lambda e, _t=t, _p=par: e.dma_start(
                    out=out[128 * _t:128 * _t + 128, :],
                    in_=ost[:, 2048 * _p:2048 * _p + 2048]),
                    waits=[(c["dve"], ov)], inc=oc, inck=16)
                B.sg([("ost", par)], oc, odv)

            if upto == "ffn":
                dump_and_finish([
                    (sff[:].bitcast(BF16),),
                    (xln2[:],),
                    (x2T[:],),
                ])
            B.emit("gpsimd", lambda e: e.nop(),
                   waits=[(c["out0"], c["out0"].v),
                          (c["out1"], c["out1"].v)])

        with nc.Block() as block:
            B.run(block)
    return nc


_NC_CACHE = {}


def _get_program():
    if "p" not in _NC_CACHE:
        _NC_CACHE["p"] = build_program()
    return _NC_CACHE["p"]


def kernel(L0_attn_x, L0_attn_w_qkv, L0_attn_w_out, L0_ffn_weight,
           L0_ffn_lora_A, L0_ffn_lora_B, L0_ffn_router_w):
    x = np.asarray(L0_attn_x, dtype=np.float32)
    wqkv_h = np.ascontiguousarray(
        np.asarray(L0_attn_w_qkv, np.float32).reshape(D, 3, H, DH)
        .transpose(2, 0, 1, 3).reshape(H, D, 384))
    w_out_a = np.ascontiguousarray(np.asarray(L0_attn_w_out, np.float32))
    weightT_a = np.ascontiguousarray(np.asarray(L0_ffn_weight, np.float32).T)
    routerT_a = np.ascontiguousarray(np.asarray(L0_ffn_router_w, np.float32).T)
    at_a = np.ascontiguousarray(
        np.asarray(L0_ffn_lora_A, np.float32).reshape(128, D).T)
    bflat_a = np.ascontiguousarray(
        np.asarray(L0_ffn_lora_B, np.float32).transpose(0, 2, 1).reshape(128, D))

    nc = _get_program()
    in_maps = []
    for cix in range(8):
        in_maps.append({
            "x": np.ascontiguousarray(x[cix]),
            "wqkv": wqkv_h, "w_out": w_out_a, "weightT": weightT_a,
            "routerT": routerT_a, "at": at_a, "bflat": bflat_a,
        })
    res = run_bass_kernel_spmd(nc, in_maps, core_ids=list(range(8)))
    return np.stack([res.results[i]["out"] for i in range(8)], axis=0)
